# version 04
# baseline (speedup 1.0000x reference)
import math
import numpy as np
import concourse.bass as bass
import concourse.mybir as mybir
from concourse.bass_utils import run_bass_kernel_spmd

F32 = mybir.dt.float32
BF16 = mybir.dt.bfloat16
AF = mybir.ActivationFunctionType
ALU = mybir.AluOpType

DBG_N = 99
DBG_NO_AG = False
DBG_NM = None
DBG_AG_MODE = 0
DBG_OUT_STAGE = 3
DBG_TAP_LN1 = False
B, S, D = 4, 2048, 4096
T = 1024
NCH = 32
NG = 2
EPS = float(np.float32(1e-5))
L1_SCALE = float(np.float32(1.0) / np.sqrt(np.float32(128.0)))
NEGBIG = -3.0e38
MASKV = -3.4028235e38

_INV_FREQ_HEX = [
    0x3F800000, 0x3F3FF911, 0x3F0FF59A, 0x3ED7E89B, 0x3EA1E89B, 0x3E72D423,
    0x3E361887, 0x3E088D77, 0x3DCCCCCD, 0x3D99940D, 0x3D6655C2, 0x3D2CBA15,
    0x3D0186E3, 0x3CC2434F, 0x3C91AD39, 0x3C5A7BF2, 0x3C23D70A, 0x3BF5B9B0,
    0x3BB8449C, 0x3B8A2E77, 0x3B4F3E38, 0x3B1B690D, 0x3AE91528, 0x3AAEC98E,
    0x3A83126F, 0x3A44948C, 0x3A136A16, 0x39DD1725, 0x39A5CB60, 0x3978A815,
    0x393A7753, 0x390BD472,
]


def _alibi_slopes(n):
    cp2 = 2 ** math.floor(math.log2(n))
    base = 2 ** (-(2 ** (-(math.log2(cp2) - 3))))
    return (base ** np.arange(1, cp2 + 1, dtype=np.float32)).astype(np.float32)


def _rope_tables():
    inv = np.array(_INV_FREQ_HEX, dtype=np.uint32).view(np.float32)
    fr = (np.arange(S, dtype=np.float32)[:, None] * inv[None, :]).astype(np.float32)
    return np.cos(fr).astype(np.float32), np.sin(fr).astype(np.float32)



class Rec:
    __slots__ = ("eng", "kind", "pos", "fn", "waits", "inc", "dmaidx", "tval")

    def __init__(self, eng, kind, pos, fn):
        self.eng, self.kind, self.pos, self.fn = eng, kind, pos, fn
        self.waits = []
        self.inc = False
        self.dmaidx = None
        self.tval = None


class Emit:
    ENGS = ("T", "V", "S", "G", "Y")

    def __init__(self):
        self.recs = {e: [] for e in self.ENGS}
        self.tcount = {e: 0 for e in self.ENGS}
        self.dcount = {e: 0 for e in self.ENGS}
        self.res = {}
        self.hw = {}
        self.barrier = None

    def _ticket(self, rec):
        if rec.kind == "d":
            return ("d%s%d" % (rec.eng, rec.dmaidx % 8), 16 * (rec.dmaidx // 8 + 1))
        if rec.inc:
            return ("c" + rec.eng, rec.tval)
        lst = self.recs[rec.eng]
        for r in lst[rec.pos:]:
            if r.kind == "c" and r.inc:
                return ("c" + rec.eng, r.tval)
        for r in reversed(lst):
            if r.kind == "c":
                r.inc = True
                self.tcount[rec.eng] += 1
                r.tval = self.tcount[rec.eng]
                return ("c" + rec.eng, r.tval)
        raise RuntimeError("no compute rec to ticket on " + rec.eng)

    def _dep(self, eng, newrec, dep):
        if dep is None or dep is newrec:
            return
        if dep.eng == eng == "T" and dep.kind == "c" and newrec.kind == "c":
            return
        sem, val = self._ticket(dep)
        key = (eng, sem)
        if self.hw.get(key, 0) >= val:
            return
        newrec.waits.append((sem, val))
        self.hw[key] = val

    def op(self, eng, fn, reads=(), writes=(), dma=False, ticket=False):
        kind = "d" if dma else "c"
        rec = Rec(eng, kind, len(self.recs[eng]), fn)
        if dma:
            rec.dmaidx = self.dcount[eng]
            self.dcount[eng] += 1
        elif ticket:
            rec.inc = True
            self.tcount[eng] += 1
            rec.tval = self.tcount[eng]
        if self.barrier is not None:
            for bb in self.barrier:
                self._dep(eng, rec, bb)
        for r in reads:
            e = self.res.get(r)
            if e and e["w"] is not None:
                self._dep(eng, rec, e["w"])
        for w in writes:
            e = self.res.get(w)
            if e:
                if e["w"] is not None:
                    self._dep(eng, rec, e["w"])
                for rd in e["r"]:
                    self._dep(eng, rec, rd)
        for r in reads:
            self.res.setdefault(r, {"w": None, "r": []})["r"].append(rec)
        for w in writes:
            self.res[w] = {"w": rec, "r": []}
        self.recs[eng].append(rec)
        return rec

    def soft_barrier(self):
        frontier = []
        for e in self.ENGS:
            got_c = got_d = False
            for r in reversed(self.recs[e]):
                if r.kind == "c" and not got_c:
                    frontier.append(r)
                    got_c = True
                elif r.kind == "d" and not got_d:
                    frontier.append(r)
                    got_d = True
                if got_c and got_d:
                    break
        self.barrier = frontier

    def replay_engine(self, en, h, sems):
        for rec in self.recs[en]:
            for semname, val in rec.waits:
                h.wait_ge(sems[semname], val)
            instr = rec.fn(h)
            if rec.kind == "d":
                instr.then_inc(sems["d%s%d" % (en, rec.dmaidx % 8)], 16)
            elif rec.inc:
                instr.then_inc(sems["c" + en], 1)



def _stage_lhsT(Wio):
    din, M = Wio.shape
    nk, nm = din // 128, M // 128
    return np.ascontiguousarray(
        Wio.reshape(nk, 128, nm, 128).transpose(2, 1, 0, 3).reshape(nm, 128, nk * 128))


def _perm_rope_rows(n_heads):
    p = []
    for h in range(n_heads):
        base = h * 128
        p += [base + i for i in range(0, 64, 2)]
        p += [base + i for i in range(1, 64, 2)]
        p += [base + i for i in range(64, 128)]
    return np.array(p, dtype=np.int64)


def stage_inputs(ins):
    g = {}
    g["w0qkv"] = _stage_lhsT(ins["L0_attn_w_attn"])
    g["b0qkv"] = np.ascontiguousarray(ins["L0_attn_b_attn"].reshape(96, 128).T)
    g["w0proj"] = _stage_lhsT(ins["L0_attn_w_proj"])
    g["b0proj"] = np.ascontiguousarray(ins["L0_attn_b_proj"].reshape(32, 128).T)
    g["w0ffn"] = np.ascontiguousarray(
        ins["L0_ffn_w"].reshape(32, 128, 8, 512).transpose(2, 1, 0, 3).reshape(8, 128, 32 * 512))
    g["b0ffn"] = np.ascontiguousarray(ins["L0_ffn_b"].reshape(32, 128).T)
    perm = np.arange(4608)
    perm[:4096] = _perm_rope_rows(32)
    perm[4096:4352] = 4096 + _perm_rope_rows(2)
    qkv_w_p = ins["L1_attn_qkv_w"][perm]
    g["w1qkv"] = _stage_lhsT(np.ascontiguousarray(qkv_w_p.T))
    g["b1qkv"] = np.ascontiguousarray(ins["L1_attn_qkv_b"][perm].reshape(36, 128).T)
    g["w1dense"] = _stage_lhsT(np.ascontiguousarray(ins["L1_attn_dense_w"].T))
    g["w1base"] = _stage_lhsT(np.ascontiguousarray(ins["L1_ffn_base_W"].T))
    g["w1rout"] = np.ascontiguousarray(ins["L1_ffn_router_W"].T.reshape(32, 128, 8))
    A_all = ins["L1_ffn_lora_A"].transpose(1, 0, 2).reshape(4096, 128)
    g["w1loraA"] = _stage_lhsT(np.ascontiguousarray(A_all))
    B_all = ins["L1_ffn_lora_B"].reshape(128, 4096)
    g["w1loraB"] = _stage_lhsT(np.ascontiguousarray(B_all))
    g["ones_col"] = np.ones((128, 1), dtype=np.float32)
    g["ones_row"] = np.ones((1, 128), dtype=np.float32)
    s2 = np.zeros((8, 128), dtype=np.float32)
    for e in range(8):
        s2[e, 16 * e:16 * (e + 1)] = 2.0
    g["s2"] = s2
    g["identf"] = np.eye(128, dtype=np.float32)
    return g


def make_dmat(q0_abs):
    qs = np.arange(8 * 128).reshape(8, 128)
    kb = np.arange(4 * 512).reshape(4, 512)
    rel = kb[None, :, None, :] - (q0_abs + qs[:, None, :, None])
    da = np.where(rel <= 0, rel, NEGBIG).astype(np.float32)
    dm = np.where(rel <= 0, 0.0, MASKV).astype(np.float32)
    return np.ascontiguousarray(da), np.ascontiguousarray(dm)


LNH = 0
LNL = 16384
EF = 32768
XC = 33792
AC = 34816
ZC = 35840
SQ = 36864
RT1 = 37376
RT2 = 37888
SH = 38400
SL = 38912
SM0 = 39424
WREG = 41472
CROW = 41472
ARENA_N = 49664

A_DM = 0
A_SR = 8192
A_EX = 10240
A_PH = 12288
A_PL = 16384
A_KH = 20480
A_KL = 21504
A_VH = 22528
A_VL = 23552
A_VTH = 24576
A_VTL = 25600
A_QH = 26624
A_QL = 26880
A_PTH = 27136
A_PTL = 27648
A_CF = 28160
A_CS = 28672
A_RC = 29696
A_RCT = 29824

F_T = 38400
F_W = 46592
F_AW = 47616
F_SG = 48128
F_MK = 48384
F_SR = 32768


def build_program():
    nc = bass.Bass("TRN2", target_bir_lowering=False, debug=False)
    E = Emit()
    AP = {}

    def din(name, shape, dtype):
        AP[name] = nc.dram_tensor(name, shape, dtype, kind="ExternalInput").ap()

    def dint(name, shape, dtype):
        AP[name] = nc.dram_tensor(name, shape, dtype).ap()

    din("x0", [D, T], F32)
    din("w0qkv", [96, 128, 4096], F32)
    din("b0qkv", [128, 96], F32)
    din("w0proj", [32, 128, 4096], F32)
    din("b0proj", [128, 32], F32)
    din("w0ffn", [8, 128, 32 * 512], F32)
    din("b0ffn", [128, 32], F32)
    din("w1qkv", [36, 128, 4096], F32)
    din("b1qkv", [128, 36], F32)
    din("w1dense", [32, 128, 4096], F32)
    din("w1base", [32, 128, 4096], F32)
    din("w1rout", [32, 128, 8], F32)
    din("w1loraA", [1, 128, 4096], F32)
    din("w1loraB", [32, 128, 128], F32)
    din("identb", [128, 128], BF16)
    din("identf", [128, 128], F32)
    din("ones_col", [128, 1], F32)
    din("ones_row", [1, 128], F32)
    din("s2", [8, 128], F32)
    din("cosmy", [64, T], F32)
    din("sinmy", [64, T], F32)
    din("dmat_a", [8, 4, 128, 512], F32)
    din("dmat_m", [8, 4, 128, 512], F32)
    AP["out"] = nc.dram_tensor("out", [D, T], F32, kind="ExternalOutput").ap()

    dint("resid1", [D, T], F32)
    dint("resid2", [D, T], F32)
    dint("aout", [D, T], F32)
    dint("qloc", [2, D, T], BF16)
    dint("ctxloc", [2, D, T], BF16)
    dint("kvloc0", [4 * 4096, T], BF16)
    dint("kvg0", [8 * 4096, T], BF16)
    dint("kvloc1", [4 * 256, T], BF16)
    dint("kvg1", [8 * 256, T], BF16)
    dint("agdbg_i", [128, 128], F32)
    dint("agdbg_o", [256, 128], F32)

    from contextlib import ExitStack
    ctx = ExitStack()

    def sbt(name, shape, dtype):
        return ctx.enter_context(nc.sbuf_tensor(name, shape, dtype))

    biases = sbt("biases", [128, 232], F32)
    identb = sbt("identb_t", [128, 128], BF16)
    identf = sbt("identf_t", [128, 128], F32)
    onec = sbt("onec", [128, 1], F32)
    oner = sbt("oner", [1, 128], F32)
    s2t = sbt("s2t", [8, 128], F32)
    costab = sbt("costab", [64, T], F32)
    sintab = sbt("sintab", [64, T], F32)
    scales0 = sbt("scales0", [128, 32], F32)
    routw = sbt("routw", [128, 512], BF16)
    arena = sbt("arena", [128, ARENA_N], F32)

    pb = [ctx.enter_context(nc.psum_tensor(f"pb{i}", [128, 512], F32)) for i in range(6)]
    pt = [ctx.enter_context(nc.psum_tensor(f"pt{i}", [128, 1024], BF16)) for i in range(2)]
    ptf = [p[:, :].bitcast(F32) for p in pt]

    sems = {}
    for e in Emit.ENGS:
        sems["c" + e] = ctx.enter_context(nc.semaphore("c" + e))
        for i in range(8):
            sems["d%s%d" % (e, i)] = ctx.enter_context(nc.semaphore("d%s%d" % (e, i)))

    def af(off, n):
        return arena[:, off:off + n]

    def ab(off, nbf):
        return arena[:, off:off + nbf // 2].bitcast(BF16)

    o = E.op

    def dma(q, out, in_, reads=(), writes=()):
        return o(q, lambda h, a=out, b=in_: h.dma_start(out=a, in_=b),
                 reads=reads, writes=writes, dma=True)

    def mm(lhsT, rhs, outp, start, stop, reads=(), writes=()):
        return o("T", lambda h, a=lhsT, b=rhs, c=outp, s1=start, s2=stop:
                 h.matmul(c, a, b, start=s1, stop=s2), reads=reads, writes=writes)

    def tpose(inp, outp, f32=False, pdim=128, reads=(), writes=()):
        idt = identf if f32 else identb
        return o("T", lambda h, a=inp, c=outp, i=idt, p=pdim: h.transpose(c, a, i[0:p, 0:p]),
                 reads=list(reads) + [("ident",)], writes=writes)

    def act(eng_dst, src, func=AF.Copy, bias=None, scale=None, accum=None,
            reads=(), writes=()):
        kw = {}
        if bias is not None:
            kw["bias"] = bias
        if scale is not None:
            kw["scale"] = scale
        if accum is not None:
            kw["accum_out"] = accum
        if func == AF.Copy and (
                (bias is not None and not isinstance(bias, float)) or
                (scale is not None and not isinstance(scale, float))):
            func = AF.Identity
        return o("S", lambda h, d=eng_dst, a=src, k=kw, f=func:
                 h.activation(out=d, in_=a, func=f, **k), reads=reads, writes=writes)

    dma("Y", biases[:, 0:96], AP["b0qkv"], writes=[("bias", 0)])
    dma("Y", biases[:, 96:128], AP["b0proj"], writes=[("bias", 1)])
    dma("Y", biases[:, 128:160], AP["b0ffn"], writes=[("bias", 2)])
    dma("Y", biases[:, 160:196], AP["b1qkv"], writes=[("bias", 3)])
    dma("Y", identb[:], AP["identb"], writes=[("ident",)])
    dma("Y", identf[:], AP["identf"], writes=[("ident",)])
    dma("Y", onec[:], AP["ones_col"], writes=[("onec",)])
    dma("Y", oner[:], AP["ones_row"], writes=[("oner",)])
    dma("Y", s2t[:], AP["s2"], writes=[("s2",)])
    dma("Y", costab[:], AP["cosmy"], writes=[("cos",)])
    dma("Y", sintab[:], AP["sinmy"], writes=[("sin",)])

    o("V", lambda h: h.memset(biases[:, 196:197], EPS), writes=[("epst",)])
    o("V", lambda h: h.tensor_scalar(out=biases[:, 200:232], in0=biases[:, 160:192],
                                     scalar1=L1_SCALE, scalar2=None, op0=ALU.mult),
      reads=[("bias", 3)], writes=[("bias", 4)])
    slopes = [float(v) for v in _alibi_slopes(32)]

    def lnx_hi(k):
        return ab(LNH + k * 512, 1024)

    def lnx_lo(k):
        return ab(LNL + k * 512, 1024)

    def stats_mms(src_key, src_tile, g, first, last):
        sq = af(SQ, 512)
        act(sq, src_tile, AF.Square, reads=[src_key], writes=[("sq",)])
        mm(onec[:], src_tile, pb[2 + 2 * g][0:1, :], first, last,
           reads=[src_key, ("onec",)], writes=[("ps", 2 + 2 * g)])
        mm(onec[:], sq, pb[3 + 2 * g][0:1, :], first, last,
           reads=[("sq",), ("onec",)], writes=[("ps", 3 + 2 * g)])

    def crow_row(i, g):
        return arena[0:1, CROW + i * T + g * 512: CROW + i * T + (g + 1) * 512]

    def stats_finalize():
        for g in range(NG):
            mrow, rrow, nrow = crow_row(0, g), crow_row(1, g), crow_row(2, g)
            trow, t2row = crow_row(3, g), crow_row(4, g)
            o("V", lambda h, d=mrow, s=pb[2 + 2 * g][0:1, :]:
              h.tensor_scalar(out=d, in0=s, scalar1=1.0 / 4096.0, scalar2=None, op0=ALU.mult),
              reads=[("ps", 2 + 2 * g)], writes=[("cm", g)])
            o("V", lambda h, d=trow, s=pb[3 + 2 * g][0:1, :]:
              h.tensor_scalar(out=d, in0=s, scalar1=1.0 / 4096.0, scalar2=None, op0=ALU.mult),
              reads=[("ps", 3 + 2 * g)], writes=[("ct", g)])
            o("V", lambda h, d=t2row, a=mrow: h.tensor_tensor(out=d, in0=a, in1=a, op=ALU.mult),
              reads=[("cm", g)], writes=[("ct2", g)])
            o("V", lambda h, d=trow, a=trow, bx=t2row: h.tensor_tensor(out=d, in0=a, in1=bx, op=ALU.subtract),
              reads=[("ct", g), ("ct2", g)], writes=[("ct", g)])
            act(rrow, trow, AF.Sqrt, bias=biases[0:1, 196:197],
                reads=[("ct", g), ("epst",)], writes=[("cr", g)])
            o("V", lambda h, d=rrow, a=rrow: h.reciprocal(out=d, in_=a),
              reads=[("cr", g)], writes=[("cr", g)])
            o("V", lambda h, d=nrow, a=mrow, bx=rrow: h.tensor_tensor(out=d, in0=a, in1=bx, op=ALU.mult),
              reads=[("cm", g), ("cr", g)], writes=[("cn", g)])
            o("V", lambda h, d=nrow, a=nrow: h.tensor_scalar(out=d, in0=a, scalar1=-1.0, scalar2=None, op0=ALU.mult),
              reads=[("cn", g)], writes=[("cn", g)])

    def bcast_rows(g):
        mm(oner[:], crow_row(1, g), pb[0][:, :], True, True,
           reads=[("cr", g), ("oner",)], writes=[("ps", 0)])
        mm(oner[:], crow_row(2, g), pb[1][:, :], True, True,
           reads=[("cn", g), ("oner",)], writes=[("ps", 1)])

    def _xread(xin_name, k, g):
        return [(xin_name, k, g)] if xin_name != "x0" else []

    def stats_pass(xin_name):
        for g in range(NG):
            for k in range(NCH):
                xc = af(XC + (k % 2) * 512, 512)
                dma("Y", xc, AP[xin_name][128 * k:128 * (k + 1), g * 512:(g + 1) * 512],
                    reads=_xread(xin_name, k, g), writes=[("xc", k % 2)])
                stats_mms(("xc", k % 2), xc, g, k == 0, k == NCH - 1)
        E.soft_barrier()

    def ln_apply(xin_name, tap=False):
        for g in range(NG):
            bcast_rows(g)
            for k in range(NCH):
                xc = af(XC + (k % 2) * 512, 512)
                zc = af(ZC + (k % 2) * 512, 512)
                dma("Y", xc, AP[xin_name][128 * k:128 * (k + 1), g * 512:(g + 1) * 512],
                    reads=_xread(xin_name, k, g), writes=[("xc", k % 2)])
                o("V", lambda h, d=zc, a=xc: h.tensor_tensor(out=d, in0=a, in1=pb[0][:, :], op=ALU.mult),
                  reads=[("xc", k % 2), ("ps", 0)], writes=[("zc", k % 2)])
                o("V", lambda h, d=zc, a=zc: h.tensor_tensor(out=d, in0=a, in1=pb[1][:, :], op=ALU.add),
                  reads=[("zc", k % 2), ("ps", 1)], writes=[("zc", k % 2)])
                hi = lnx_hi(k)[:, g * 512:(g + 1) * 512]
                lo = lnx_lo(k)[:, g * 512:(g + 1) * 512]
                act(hi, zc, reads=[("zc", k % 2)], writes=[("lnxh", k, g)])
                o("G", lambda h, d=lo, a=zc, bx=hi: h.tensor_sub(d, a, bx),
                  reads=[("zc", k % 2), ("lnxh", k, g)], writes=[("lnxl", k, g)])
                if tap:
                    dma("G", AP["out"][128 * k:128 * (k + 1), g * 512:(g + 1) * 512], zc,
                        reads=[("zc", k % 2)], writes=[("out", k, g)])
        E.soft_barrier()

    def residual_pass(xin_name, xout_name, do_next_stats):
        stats_finalize()
        for g in range(NG):
            bcast_rows(g)
            for k in range(NCH):
                xc = af(XC + (k % 2) * 512, 512)
                ac = af(AC + (k % 2) * 512, 512)
                zc = af(ZC + (k % 2) * 512, 512)
                dma("Y", xc, AP[xin_name][128 * k:128 * (k + 1), g * 512:(g + 1) * 512],
                    reads=_xread(xin_name, k, g), writes=[("xc", k % 2)])
                dma("Y", ac, AP["aout"][128 * k:128 * (k + 1), g * 512:(g + 1) * 512],
                    reads=[("aout", k, g)], writes=[("acr", k % 2)])
                o("V", lambda h, d=zc, a=ac: h.tensor_tensor(out=d, in0=a, in1=pb[0][:, :], op=ALU.mult),
                  reads=[("acr", k % 2), ("ps", 0)], writes=[("zc", k % 2)])
                o("V", lambda h, d=zc, a=zc: h.tensor_tensor(out=d, in0=a, in1=pb[1][:, :], op=ALU.add),
                  reads=[("zc", k % 2), ("ps", 1)], writes=[("zc", k % 2)])
                o("V", lambda h, d=zc, a=zc, bx=xc: h.tensor_tensor(out=d, in0=a, in1=bx, op=ALU.add),
                  reads=[("zc", k % 2), ("xc", k % 2)], writes=[("zc", k % 2)])
                dma("G", AP[xout_name][128 * k:128 * (k + 1), g * 512:(g + 1) * 512], zc,
                    reads=[("zc", k % 2)], writes=[(xout_name, k, g)])
                if do_next_stats:
                    stats_mms(("zc", k % 2), zc, g, k == 0, k == NCH - 1)
        E.soft_barrier()

    def stream_w(wname, m):
        wf = af(WREG, 4096)
        whi = ab(WREG + 4096, 4096)
        wlo = ab(WREG + 6144, 4096)
        for q in range(4):
            s = q * 1024
            dma("Y", wf[:, s:s + 1024], AP[wname][m, :, s:s + 1024], writes=[("wf", q)])
            act(whi[:, s:s + 1024], wf[:, s:s + 1024],
                reads=[("wf", q)], writes=[("whi", q)])
            o("G", lambda h, d=wlo[:, s:s + 1024], a=wf[:, s:s + 1024], bx=whi[:, s:s + 1024]:
              h.tensor_sub(d, a, bx), reads=[("wf", q), ("whi", q)], writes=[("wlo", q)])
        return whi, wlo

    def trigger_ag(layer):
        kvloc = "kvloc0" if layer == 0 else "kvloc1"
        kvg = "kvg0" if layer == 0 else "kvg1"
        if DBG_AG_MODE == 2:
            o("G", lambda h: h.collective_compute(
                "AllGather", ALU.bypass,
                replica_groups=[[0, 1], [2, 3], [4, 5], [6, 7]],
                ins=[AP["agdbg_i"].opt()], outs=[AP["agdbg_o"].opt()]),
              reads=(), writes=[(kvg,)], ticket=True)
            return
        reads = [k for k in list(E.res) if isinstance(k, tuple) and k and k[0] == kvloc]
        o("G", lambda h, i=AP[kvloc], ot=AP[kvg]:
          h.collective_compute("AllGather", ALU.bypass,
                               replica_groups=[[0, 1], [2, 3], [4, 5], [6, 7]],
                               ins=[i.opt()], outs=[ot.opt()]),
          reads=reads, writes=[(kvg,)], ticket=True)

    def qkv_sweep(layer):
        if layer == 0:
            wname, bias_col0, q_scale = "w0qkv", 0, 1.0 / 128.0
            order = list(range(32, 96)) + list(range(0, 32))
            k_lo, k_hi_, vbase, krows, kvloc = 32, 64, 64, 4096, "kvloc0"
            ag_at = 63
        else:
            wname, bias_col0, q_scale = "w1qkv", 160, L1_SCALE
            order = list(range(32, 36)) + list(range(0, 32))
            k_lo, k_hi_, vbase, krows, kvloc = 32, 34, 34, 256, "kvloc1"
            ag_at = 3
        for mi, m in enumerate(order):
            whi, wlo = stream_w(wname, m)
            is_q = m < 32
            is_k = k_lo <= m < k_hi_
            for g in range(NG):
                bank = pb[mi % 2]
                for k in range(NCH):
                    lh = lnx_hi(k)[:, g * 512:(g + 1) * 512]
                    ll = lnx_lo(k)[:, g * 512:(g + 1) * 512]
                    wh = whi[:, 128 * k:128 * (k + 1)]
                    wl = wlo[:, 128 * k:128 * (k + 1)]
                    rd = [("lnxh", k, g), ("lnxl", k, g), ("whi", k // 8), ("wlo", k // 8)]
                    mm(wh, lh, bank[:, :], k == 0, False, reads=rd, writes=[("ps", mi % 2)])
                    mm(wh, ll, bank[:, :], False, False, reads=rd, writes=[("ps", mi % 2)])
                    mm(wl, lh, bank[:, :], False, k == NCH - 1, reads=rd, writes=[("ps", mi % 2)])
                ef = af(EF + (g % 2) * 512, 512)
                sc = q_scale if is_q else None
                if is_q and layer == 1:
                    bsl = biases[:, 200 + m:200 + m + 1]
                    brd = ("bias", 4)
                else:
                    bsl = biases[:, bias_col0 + m:bias_col0 + m + 1]
                    brd = ("bias", 0)
                act(ef, bank[:, :], bias=bsl, scale=sc,
                    reads=[("ps", mi % 2), brd, ("bias", 3)], writes=[("ef", g % 2)])
                if layer == 1 and (is_q or is_k):
                    ct1 = costab[0:32, g * 512:(g + 1) * 512]
                    ct2 = costab[32:64, g * 512:(g + 1) * 512]
                    st1 = sintab[0:32, g * 512:(g + 1) * 512]
                    st2 = sintab[32:64, g * 512:(g + 1) * 512]
                    x1, x2 = ef[0:32, :], ef[32:64, :]
                    t1 = af(RT1, 512)[0:32, :]
                    t2 = af(RT1, 512)[32:64, :]
                    t3 = af(RT2, 512)[0:32, :]
                    t4 = af(RT2, 512)[32:64, :]
                    o("V", lambda h, d=t1, a=x1, c=ct1: h.tensor_tensor(out=d, in0=a, in1=c, op=ALU.mult),
                      reads=[("ef", g % 2), ("cos",)], writes=[("rt", 1)])
                    o("V", lambda h, d=t2, a=x2, c=ct2: h.tensor_tensor(out=d, in0=a, in1=c, op=ALU.mult),
                      reads=[("ef", g % 2), ("cos",)], writes=[("rt", 2)])
                    o("V", lambda h, d=t3, a=x2, c=st2: h.tensor_tensor(out=d, in0=a, in1=c, op=ALU.mult),
                      reads=[("ef", g % 2), ("sin",)], writes=[("rt", 3)])
                    o("V", lambda h, d=t4, a=x1, c=st1: h.tensor_tensor(out=d, in0=a, in1=c, op=ALU.mult),
                      reads=[("ef", g % 2), ("sin",)], writes=[("rt", 4)])
                    o("V", lambda h, d=x1, a=t1, bx=t3: h.tensor_tensor(out=d, in0=a, in1=bx, op=ALU.subtract),
                      reads=[("rt", 1), ("rt", 3)], writes=[("ef", g % 2)])
                    o("V", lambda h, d=x2, a=t2, bx=t4: h.tensor_tensor(out=d, in0=a, in1=bx, op=ALU.add),
                      reads=[("rt", 2), ("rt", 4), ("ef", g % 2)], writes=[("ef", g % 2)])
                hi = ab(SH + (g % 2) * 256, 512)
                lo = ab(SL + (g % 2) * 256, 512)
                act(hi, ef, reads=[("ef", g % 2)], writes=[("sh", g % 2)])
                o("G", lambda h, d=lo, a=ef, bx=hi: h.tensor_sub(d, a, bx),
                  reads=[("ef", g % 2), ("sh", g % 2)], writes=[("sl", g % 2)])
                cs = slice(g * 512, (g + 1) * 512)
                if is_q:
                    dma("G", AP["qloc"][0, 128 * m:128 * (m + 1), cs], hi,
                        reads=[("sh", g % 2)], writes=[("qloc", 0, m, g)])
                    dma("G", AP["qloc"][1, 128 * m:128 * (m + 1), cs], lo,
                        reads=[("sl", g % 2)], writes=[("qloc", 1, m, g)])
                elif is_k:
                    r0 = 128 * (m - k_lo)
                    dma("G", AP[kvloc][r0:r0 + 128, cs], hi,
                        reads=[("sh", g % 2)], writes=[(kvloc, 0, m, g)])
                    dma("G", AP[kvloc][krows + r0:krows + r0 + 128, cs], lo,
                        reads=[("sl", g % 2)], writes=[(kvloc, 1, m, g)])
                else:
                    r0 = 128 * (m - vbase)
                    dma("G", AP[kvloc][2 * krows + r0:2 * krows + r0 + 128, cs], hi,
                        reads=[("sh", g % 2)], writes=[(kvloc, 2, m, g)])
                    dma("G", AP[kvloc][3 * krows + r0:3 * krows + r0 + 128, cs], lo,
                        reads=[("sl", g % 2)], writes=[(kvloc, 3, m, g)])
            if mi == (0 if DBG_AG_MODE == 1 else ag_at) and not DBG_NO_AG:
                trigger_ag(layer)
            if DBG_NM is not None and mi >= DBG_NM:
                break
        E.soft_barrier()

    def attention(layer):
        kvg = AP["kvg0"] if layer == 0 else AP["kvg1"]
        kvgk = ("kvg0",) if layer == 0 else ("kvg1",)
        krows = 4096 if layer == 0 else 256
        dmat = AP["dmat_a"] if layer == 0 else AP["dmat_m"]
        for qst in range(NG):
            dm = af(A_DM, 8192)
            for qs in range(4):
                dma("Y", dm[:, 2048 * qs:2048 * (qs + 1)].rearrange("p (k j) -> p k j", k=4),
                    dmat[4 * qst + qs].rearrange("k p j -> p k j"),
                    writes=[("dm", qs)])
            for h_ in range(32):
                kvh = h_ if layer == 0 else h_ // 16
                new_kv = (layer == 0) or (h_ % 16 == 0)
                kh, kl = ab(A_KH, 2048), ab(A_KL, 2048)
                vh, vl = ab(A_VH, 2048), ab(A_VL, 2048)
                vth, vtl = ab(A_VTH, 2048), ab(A_VTL, 2048)
                if new_kv:
                    for rb in range(2):
                        cs = slice(1024 * rb, 1024 * (rb + 1))
                        r0 = 4 * krows * rb + 128 * kvh
                        dma("Y", kh[:, cs], kvg[r0:r0 + 128, :], reads=[kvgk], writes=[("kh", rb)])
                        dma("Y", kl[:, cs], kvg[krows + r0:krows + r0 + 128, :], reads=[kvgk], writes=[("kl", rb)])
                        dma("Y", vh[:, cs], kvg[2 * krows + r0:2 * krows + r0 + 128, :], reads=[kvgk], writes=[("vh", rb)])
                        dma("Y", vl[:, cs], kvg[3 * krows + r0:3 * krows + r0 + 128, :], reads=[kvgk], writes=[("vl", rb)])
                    for half in range(4):
                        for j in range(4):
                            t = 4 * half + j
                            tpose(vh[:, 128 * t:128 * (t + 1)], pt[0][:, 128 * j:128 * (j + 1)],
                                  reads=[("vh", t // 8)], writes=[("pt0", j)])
                            tpose(vl[:, 128 * t:128 * (t + 1)], pt[1][:, 128 * j:128 * (j + 1)],
                                  reads=[("vl", t // 8)], writes=[("pt1", j)])
                        act(vth[:, 512 * half:512 * (half + 1)], pt[0][:, 0:512],
                            reads=[("pt0", 0), ("pt0", 1), ("pt0", 2), ("pt0", 3)],
                            writes=[("vth", half)])
                        act(vtl[:, 512 * half:512 * (half + 1)], pt[1][:, 0:512],
                            reads=[("pt1", 0), ("pt1", 1), ("pt1", 2), ("pt1", 3)],
                            writes=[("vtl", half)])
                qh, ql = ab(A_QH, 512), ab(A_QL, 512)
                cs = slice(qst * 512, (qst + 1) * 512)
                dma("Y", qh, AP["qloc"][0, 128 * h_:128 * (h_ + 1), cs],
                    reads=[("qloc", 0, h_, qst)], writes=[("qh",)])
                dma("Y", ql, AP["qloc"][1, 128 * h_:128 * (h_ + 1), cs],
                    reads=[("qloc", 1, h_, qst)], writes=[("ql",)])

                for qs in range(4):
                    sr = af(A_SR, 2048)
                    qhs = qh[:, 128 * qs:128 * (qs + 1)]
                    qls = ql[:, 128 * qs:128 * (qs + 1)]
                    for kb in range(4):
                        bank = pb[kb % 2]
                        rd = [("qh",), ("ql",), ("kh", kb // 2), ("kl", kb // 2)]
                        ksl = slice(512 * kb, 512 * (kb + 1))
                        mm(qhs, kh[:, ksl], bank[:, :], True, False, reads=rd, writes=[("ps", kb % 2)])
                        mm(qhs, kl[:, ksl], bank[:, :], False, False, reads=rd, writes=[("ps", kb % 2)])
                        mm(qls, kh[:, ksl], bank[:, :], False, True, reads=rd, writes=[("ps", kb % 2)])
                        dsl = dm[:, 2048 * qs + 512 * kb:2048 * qs + 512 * (kb + 1)]
                        slp = slopes[h_] if layer == 0 else 1.0
                        o("V", lambda h, d=sr[:, ksl], a=dsl, s=slp, bk=bank:
                          h.scalar_tensor_tensor(out=d, in0=a, scalar=s, in1=bk[:, :],
                                                 op0=ALU.mult, op1=ALU.add),
                          reads=[("dm", qs), ("ps", kb % 2)], writes=[("sr", kb)])
                    mx = af(A_RC + 8, 1)
                    nmx = af(A_RC + 16, 1)
                    o("V", lambda h, d=mx, a=sr: h.tensor_reduce(out=d, in_=a, axis=mybir.AxisListType.X, op=ALU.max),
                      reads=[("sr", 0), ("sr", 1), ("sr", 2), ("sr", 3)], writes=[("mx",)])
                    o("V", lambda h, d=nmx, a=mx: h.tensor_scalar(out=d, in0=a, scalar1=-1.0, scalar2=None, op0=ALU.mult),
                      reads=[("mx",)], writes=[("nmx",)])
                    ex = af(A_EX, 2048)
                    sm = af(A_RC + 24, 1)
                    act(ex, sr, AF.Exp, bias=nmx, accum=sm,
                        reads=[("sr", 0), ("sr", 1), ("sr", 2), ("sr", 3), ("nmx",)],
                        writes=[("ex",), ("sm",)])
                    rc = af(A_RC, 4)[:, qs:qs + 1]
                    o("V", lambda h, d=rc, a=sm: h.reciprocal(out=d, in_=a),
                      reads=[("sm",)], writes=[("rc", qs)])
                    phi = ab(A_PH + qs * 1024, 2048)
                    plo = ab(A_PL + qs * 1024, 2048)
                    act(phi, ex, reads=[("ex",)], writes=[("phi", qs)])
                    o("G", lambda h, d=plo, a=ex, bx=phi: h.tensor_sub(d, a, bx),
                      reads=[("ex",), ("phi", qs)], writes=[("plo", qs)])

                for qs in range(4):
                    tpose(af(A_RC, 4)[:, qs:qs + 1], pb[4][0:1, 128 * qs:128 * (qs + 1)],
                          f32=True, reads=[("rc", qs)], writes=[("ps", 4)])
                rct = af(A_RCT, 512)[0:1, :]
                o("V", lambda h, d=rct, a=pb[4][0:1, 0:512]: h.tensor_copy(out=d, in_=a),
                  reads=[("ps", 4)], writes=[("rct",)])
                mm(oner[:], rct, pb[4][:, :], True, True,
                   reads=[("rct",), ("oner",)], writes=[("ps", 4)])

                for t in range(16):
                    pth = ab(A_PTH + (t % 2) * 256, 512)
                    ptl = ab(A_PTL + (t % 2) * 256, 512)
                    for qs in range(4):
                        phi = ab(A_PH + qs * 1024, 2048)
                        plo = ab(A_PL + qs * 1024, 2048)
                        tpose(phi[:, 128 * t:128 * (t + 1)], pt[0][:, 128 * qs:128 * (qs + 1)],
                              reads=[("phi", qs)], writes=[("pt0", qs)])
                        tpose(plo[:, 128 * t:128 * (t + 1)], pt[1][:, 128 * qs:128 * (qs + 1)],
                              reads=[("plo", qs)], writes=[("pt1", qs)])
                    act(pth, pt[0][:, 0:512],
                        reads=[("pt0", 0), ("pt0", 1), ("pt0", 2), ("pt0", 3)], writes=[("pth", t % 2)])
                    o("V", lambda h, d=ptl, a=pt[1][:, 0:512]: h.tensor_copy(out=d, in_=a),
                      reads=[("pt1", 0), ("pt1", 1), ("pt1", 2), ("pt1", 3)], writes=[("ptl", t % 2)])
                    rd = [("pth", t % 2), ("ptl", t % 2), ("vth", t // 4), ("vtl", t // 4)]
                    vsl = slice(128 * t, 128 * (t + 1))
                    mm(vth[:, vsl], pth, pb[5][:, :], t == 0, False, reads=rd, writes=[("ps", 5)])
                    mm(vth[:, vsl], ptl, pb[5][:, :], False, False, reads=rd, writes=[("ps", 5)])
                    mm(vtl[:, vsl], pth, pb[5][:, :], False, t == 15, reads=rd, writes=[("ps", 5)])
                cf = af(A_CF, 512)
                act(cf, pb[5][:, :], reads=[("ps", 5)], writes=[("cf",)])
                o("V", lambda h, d=cf, a=cf: h.tensor_tensor(out=d, in0=a, in1=pb[4][:, :], op=ALU.mult),
                  reads=[("cf",), ("ps", 4)], writes=[("cf",)])
                chs = ab(A_CS, 512)
                cls_ = ab(A_CS + 256, 512)
                act(chs, cf, reads=[("cf",)], writes=[("chs",)])
                o("G", lambda h, d=cls_, a=cf, bx=chs: h.tensor_sub(d, a, bx),
                  reads=[("cf",), ("chs",)], writes=[("cls",)])
                dma("G", AP["ctxloc"][0, 128 * h_:128 * (h_ + 1), cs], chs,
                    reads=[("chs",)], writes=[("ctxloc", 0, h_, qst)])
                dma("G", AP["ctxloc"][1, 128 * h_:128 * (h_ + 1), cs], cls_,
                    reads=[("cls",)], writes=[("ctxloc", 1, h_, qst)])
        E.soft_barrier()

    def proj_sweep(wname, nm, load_ctx, bias_col0, bias_idx, extra_mm=None,
                   do_stats=True):
        if load_ctx:
            for k in range(NCH):
                dma("Y", lnx_hi(k), AP["ctxloc"][0, 128 * k:128 * (k + 1), :],
                    reads=[("ctxloc", 0, k, 0), ("ctxloc", 0, k, 1)],
                    writes=[("lnxh", k, 0), ("lnxh", k, 1)])
                dma("Y", lnx_lo(k), AP["ctxloc"][1, 128 * k:128 * (k + 1), :],
                    reads=[("ctxloc", 1, k, 0), ("ctxloc", 1, k, 1)],
                    writes=[("lnxl", k, 0), ("lnxl", k, 1)])
        for m in range(nm):
            whi, wlo = stream_w(wname, m)
            if extra_mm is not None:
                extra_setup = extra_mm(m, None, None)
            for g in range(NG):
                bank = pb[m % 2]
                for k in range(NCH):
                    lh = lnx_hi(k)[:, g * 512:(g + 1) * 512]
                    ll = lnx_lo(k)[:, g * 512:(g + 1) * 512]
                    wh = whi[:, 128 * k:128 * (k + 1)]
                    wl = wlo[:, 128 * k:128 * (k + 1)]
                    rd = [("lnxh", k, g), ("lnxl", k, g), ("whi", k // 8), ("wlo", k // 8)]
                    mm(wh, lh, bank[:, :], k == 0, False, reads=rd, writes=[("ps", m % 2)])
                    mm(wh, ll, bank[:, :], False, False, reads=rd, writes=[("ps", m % 2)])
                    last = (k == NCH - 1) and extra_mm is None
                    mm(wl, lh, bank[:, :], False, last, reads=rd, writes=[("ps", m % 2)])
                if extra_mm is not None:
                    extra_mm(m, g, bank)
                ef = af(EF + (g % 2) * 512, 512)
                bsl = biases[:, bias_col0 + m:bias_col0 + m + 1] if bias_col0 is not None else None
                act(ef, bank[:, :], bias=bsl,
                    reads=[("ps", m % 2)] + ([("bias", bias_idx)] if bias_idx is not None else []),
                    writes=[("ef", g % 2)])
                dma("G", AP["aout"][128 * m:128 * (m + 1), g * 512:(g + 1) * 512], ef,
                    reads=[("ef", g % 2)], writes=[("aout", m, g)])
                if do_stats:
                    stats_mms(("ef", g % 2), ef, g, m == 0, m == nm - 1)
        E.soft_barrier()

    def ffn0():
        W = AP["w0ffn"]
        for mg in range(8):
            for k in range(NCH):
                wr = af(F_W + (k % 2) * 512, 512)
                dma("Y", wr, W[mg, :, 512 * k:512 * (k + 1)], writes=[("w0r", k % 2)])
                aw = af(F_AW, 512)
                act(aw, wr, AF.Abs, reads=[("w0r", k % 2)], writes=[("aw",)])
                mm(onec[:], aw, pb[2][0:1, :], k == 0, k == NCH - 1,
                   reads=[("aw",), ("onec",)], writes=[("ps", 2)])
            srow = arena[0:1, F_SR:F_SR + 512]
            o("V", lambda h, d=srow, a=pb[2][0:1, :]: h.tensor_copy(out=d, in_=a),
              reads=[("ps", 2)], writes=[("srow",)])
            for j in range(4):
                tpose(srow[0:1, 128 * j:128 * (j + 1)], pb[3][:, 4 * mg + j:4 * mg + j + 1],
                      f32=True, pdim=1, reads=[("srow",)], writes=[("ps3c", mg, j)])
        act(scales0[:], pb[3][:, 0:32], scale=1.0 / 4096.0,
            reads=[("ps3c", mg2, j2) for mg2 in range(8) for j2 in range(4)],
            writes=[("scales",)])
        E.soft_barrier()
        banks = [pb[0][:, :], pb[1][:, :], pb[2][:, :], pb[3][:, :],
                 pb[4][:, :], pb[5][:, :], ptf[0][:, 0:512], ptf[1][:, 0:512]]
        bkey = [("ps", 0), ("ps", 1), ("ps", 2), ("ps", 3), ("ps", 4), ("ps", 5),
                ("pt0", 0), ("pt1", 0)]
        for mg in range(8):
            for k in range(NCH):
                wr = af(F_W + (k % 2) * 512, 512)
                dma("Y", wr, W[mg, :, 512 * k:512 * (k + 1)], writes=[("w0r", k % 2)])
                aw = af(F_AW, 512)
                sgn = ab(F_SG, 512)
                msk = ab(F_MK, 512)
                tk = ab(F_T + (k % 4) * 256, 512)
                act(aw, wr, AF.Abs, reads=[("w0r", k % 2)], writes=[("aw",)])
                act(sgn, wr, AF.Sign, reads=[("w0r", k % 2)], writes=[("sgn",)])
                o("V", lambda h, d=msk, a=aw: h.tensor_scalar(out=d, in0=a, scalar1=0.05, scalar2=None, op0=ALU.is_ge),
                  reads=[("aw",)], writes=[("msk",)])
                o("V", lambda h, d=tk, a=sgn, bx=msk: h.tensor_tensor(out=d, in0=a, in1=bx, op=ALU.mult),
                  reads=[("sgn",), ("msk",)], writes=[("tk", k % 4)])
                for j in range(4):
                    tsl = tk[:, 128 * j:128 * (j + 1)]
                    for g in range(NG):
                        bi = 4 * g + j
                        lh = lnx_hi(k)[:, g * 512:(g + 1) * 512]
                        ll = lnx_lo(k)[:, g * 512:(g + 1) * 512]
                        rd = [("lnxh", k, g), ("lnxl", k, g), ("tk", k % 4)]
                        mm(tsl, lh, banks[bi], k == 0, False, reads=rd, writes=[bkey[bi]])
                        mm(tsl, ll, banks[bi], False, k == NCH - 1, reads=rd, writes=[bkey[bi]])
            for j in range(4):
                for g in range(NG):
                    bi = 4 * g + j
                    m = 4 * mg + j
                    ef = af(EF, 512)
                    act(ef, banks[bi], bias=biases[:, 128 + m:129 + m],
                        scale=scales0[:, m:m + 1],
                        reads=[bkey[bi], ("scales",), ("bias", 2)], writes=[("ef", 0)])
                    dma("G", AP["aout"][128 * m:128 * (m + 1), g * 512:(g + 1) * 512], ef,
                        reads=[("ef", 0)], writes=[("aout", m, g)])
        E.soft_barrier()

    def ffn1():
        rws = af(ZC, 128)
        for k in range(NCH):
            dma("Y", rws[:, 8 * (k % 4):8 * (k % 4) + 8], AP["w1rout"][k],
                writes=[("rwf", k % 4)])
            rh = routw[:, 8 * k:8 * (k + 1)]
            rl = routw[:, 256 + 8 * k:264 + 8 * k]
            act(rh, rws[:, 8 * (k % 4):8 * (k % 4) + 8],
                reads=[("rwf", k % 4)], writes=[("rwh", k)])
            o("G", lambda h, d=rl, a=rws[:, 8 * (k % 4):8 * (k % 4) + 8], bx=rh:
              h.tensor_sub(d, a, bx), reads=[("rwf", k % 4), ("rwh", k)], writes=[("rwl", k)])
        whiA, wloA = stream_w("w1loraA", 0)
        for g in range(NG):
            for k in range(NCH):
                lh = lnx_hi(k)[:, g * 512:(g + 1) * 512]
                ll = lnx_lo(k)[:, g * 512:(g + 1) * 512]
                rh = routw[:, 8 * k:8 * (k + 1)]
                rl = routw[:, 256 + 8 * k:264 + 8 * k]
                rd = [("lnxh", k, g), ("lnxl", k, g), ("rwh", k), ("rwl", k)]
                mm(rh, lh, pb[0][0:8, :], k == 0, False, reads=rd, writes=[("ps", 0)])
                mm(rh, ll, pb[0][0:8, :], False, False, reads=rd, writes=[("ps", 0)])
                mm(rl, lh, pb[0][0:8, :], False, k == NCH - 1, reads=rd, writes=[("ps", 0)])
            lg = af(SM0, 512)[0:8, :]
            o("V", lambda h, d=lg, a=pb[0][0:8, :]: h.tensor_copy(out=d, in_=a),
              reads=[("ps", 0)], writes=[("lg",)])
            for tt in range(4):
                tpose(lg[:, 128 * tt:128 * (tt + 1)], pb[1][:, 8 * tt:8 * (tt + 1)],
                      f32=True, pdim=8, reads=[("lg",)], writes=[("lgT", tt)])
            lgt = af(SM0 + 512, 32)
            o("V", lambda h, d=lgt, a=pb[1][:, 0:32]: h.tensor_copy(out=d, in_=a),
              reads=[("lgT", 0), ("lgT", 1), ("lgT", 2), ("lgT", 3)], writes=[("lgt",)])
            wdn = af(SM0 + 544, 32)
            sc0 = SM0 + 576
            for tt in range(4):
                sl_ = lgt[:, 8 * tt:8 * (tt + 1)]
                m1 = af(sc0, 1)
                nm1 = af(sc0 + 1, 1)
                o("V", lambda h, d=m1, a=sl_: h.tensor_reduce(out=d, in_=a, axis=mybir.AxisListType.X, op=ALU.max),
                  reads=[("lgt",)], writes=[("m1",)])
                o("V", lambda h, d=nm1, a=m1: h.tensor_scalar(out=d, in0=a, scalar1=-1.0, scalar2=None, op0=ALU.mult),
                  reads=[("m1",)], writes=[("nm1",)])
                dmx = af(sc0 + 8, 8)
                o("V", lambda h, d=dmx, a=sl_, bx=nm1: h.tensor_scalar(out=d, in0=a, scalar1=bx, scalar2=None, op0=ALU.add),
                  reads=[("lgt",), ("nm1",)], writes=[("dmx",)])
                mk2 = af(sc0 + 16, 8)
                o("V", lambda h, d=mk2, a=dmx: h.tensor_scalar(out=d, in0=a, scalar1=0.0, scalar2=-1e30, op0=ALU.is_ge, op1=ALU.mult),
                  reads=[("dmx",)], writes=[("mk2",)])
                l2 = af(sc0 + 24, 8)
                o("V", lambda h, d=l2, a=mk2, bx=sl_: h.tensor_tensor(out=d, in0=a, in1=bx, op=ALU.add),
                  reads=[("mk2",), ("lgt",)], writes=[("l2",)])
                m2 = af(sc0 + 32, 1)
                nm2 = af(sc0 + 33, 1)
                o("V", lambda h, d=m2, a=l2: h.tensor_reduce(out=d, in_=a, axis=mybir.AxisListType.X, op=ALU.max),
                  reads=[("l2",)], writes=[("m2",)])
                o("V", lambda h, d=nm2, a=m2: h.tensor_scalar(out=d, in0=a, scalar1=-1.0, scalar2=None, op0=ALU.mult),
                  reads=[("m2",)], writes=[("nm2",)])
                e_ = af(sc0 + 40, 8)
                act(e_, sl_, AF.Exp, bias=nm1, reads=[("lgt",), ("nm1",)], writes=[("e_",)])
                e2 = af(sc0 + 48, 1)
                act(e2, m2, AF.Exp, bias=nm1, reads=[("m2",), ("nm1",)], writes=[("e2",)])
                den = af(sc0 + 49, 1)
                o("V", lambda h, d=den, a=e2: h.tensor_scalar(out=d, in0=a, scalar1=1.0, scalar2=None, op0=ALU.add),
                  reads=[("e2",)], writes=[("den",)])
                o("V", lambda h, d=den, a=den: h.reciprocal(out=d, in_=a),
                  reads=[("den",)], writes=[("den",)])
                gem = af(sc0 + 56, 8)
                o("V", lambda h, d=gem, a=sl_, bx=nm2: h.tensor_scalar(out=d, in0=a, scalar1=bx, scalar2=0.0, op0=ALU.add, op1=ALU.is_ge),
                  reads=[("lgt",), ("nm2",)], writes=[("gem",)])
                we = af(sc0 + 64, 8)
                o("V", lambda h, d=we, a=e_, bx=gem: h.tensor_tensor(out=d, in0=a, in1=bx, op=ALU.mult),
                  reads=[("e_",), ("gem",)], writes=[("we",)])
                o("V", lambda h, d=wdn[:, 8 * tt:8 * (tt + 1)], a=we, bx=den:
                  h.tensor_scalar(out=d, in0=a, scalar1=bx, scalar2=None, op0=ALU.mult),
                  reads=[("we",), ("den",)], writes=[("wdn", tt)])
            for tt in range(4):
                tpose(wdn[:, 8 * tt:8 * (tt + 1)], pb[1][0:8, 128 * tt:128 * (tt + 1)],
                      f32=True, reads=[("wdn", tt)], writes=[("wdnT", tt)])
            wdt = af(SM0 + 640, 512)[0:8, :]
            o("V", lambda h, d=wdt, a=pb[1][0:8, 0:512]: h.tensor_copy(out=d, in_=a),
              reads=[("wdnT", 0), ("wdnT", 1), ("wdnT", 2), ("wdnT", 3)], writes=[("wdt",)])
            mm(s2t[:], wdt, pb[2][:, :], True, True,
               reads=[("wdt",), ("s2",)], writes=[("ps", 2)])
            for k in range(NCH):
                lh = lnx_hi(k)[:, g * 512:(g + 1) * 512]
                ll = lnx_lo(k)[:, g * 512:(g + 1) * 512]
                wh = whiA[:, 128 * k:128 * (k + 1)]
                wl = wloA[:, 128 * k:128 * (k + 1)]
                rd = [("lnxh", k, g), ("lnxl", k, g), ("whi", k // 8), ("wlo", k // 8)]
                mm(wh, lh, pb[3][:, :], k == 0, False, reads=rd, writes=[("ps", 3)])
                mm(wh, ll, pb[3][:, :], False, False, reads=rd, writes=[("ps", 3)])
                mm(wl, lh, pb[3][:, :], False, k == NCH - 1, reads=rd, writes=[("ps", 3)])
            mid = af(XC, 512)
            act(mid, pb[3][:, :], reads=[("ps", 3)], writes=[("mid",)])
            o("V", lambda h, d=mid, a=mid: h.tensor_tensor(out=d, in0=a, in1=pb[2][:, :], op=ALU.mult),
              reads=[("mid",), ("ps", 2)], writes=[("mid",)])
            mh = ab(SQ, 1024)[:, g * 512:(g + 1) * 512]
            ml = ab(RT1, 1024)[:, g * 512:(g + 1) * 512]
            act(mh, mid, reads=[("mid",)], writes=[("mh", g)])
            o("G", lambda h, d=ml, a=mid, bx=mh: h.tensor_sub(d, a, bx),
              reads=[("mid",), ("mh", g)], writes=[("ml", g)])
        E.soft_barrier()

        lbh = ab(AC, 512)[:, 0:128]
        lbl = ab(AC + 128, 512)[:, 0:128]

        def lora_extra(m, g, bank):
            if g is None:
                lbf = af(ZC, 128)
                dma("Y", lbf, AP["w1loraB"][m], writes=[("lbf",)])
                act(lbh, lbf, reads=[("lbf",)], writes=[("lbh",)])
                o("G", lambda h, d=lbl, a=lbf, bx=lbh: h.tensor_sub(d, a, bx),
                  reads=[("lbf",), ("lbh",)], writes=[("lbl",)])
                return
            mh = ab(SQ, 1024)[:, g * 512:(g + 1) * 512]
            ml = ab(RT1, 1024)[:, g * 512:(g + 1) * 512]
            rd = [("mh", g), ("ml", g), ("lbh",), ("lbl",)]
            mm(lbh, mh, bank[:, :], False, False, reads=rd, writes=[("ps", m % 2)])
            mm(lbh, ml, bank[:, :], False, False, reads=rd, writes=[("ps", m % 2)])
            mm(lbl, mh, bank[:, :], False, True, reads=rd, writes=[("ps", m % 2)])

        proj_sweep("w1base", 32, False, None, None, extra_mm=lora_extra, do_stats=False)
        stats_pass("aout")

    steps = []
    steps.append(lambda: (stats_pass("x0"), stats_finalize(), ln_apply("x0", tap=DBG_TAP_LN1)))
    resids = ["resid1", "resid2", "resid1", "resid2"]
    resids[DBG_OUT_STAGE] = "out"
    state = {"xin": "x0", "ri": 0}

    def mk_layer(layer):
        def attn_part1():
            qkv_sweep(layer)
        def attn_part2():
            attention(layer)
        def attn_part3():
            proj_sweep("w0proj" if layer == 0 else "w1dense", 32, True,
                       96 if layer == 0 else None, 1 if layer == 0 else None,
                       do_stats=True)
        def attn_part4():
            xout = resids[state["ri"]]; state["ri"] += 1
            residual_pass(state["xin"], xout, do_next_stats=True)
            stats_finalize()
            ln_apply(xout)
            state["xin"] = xout
        def ffn_part():
            if layer == 0:
                ffn0()
                stats_pass("aout")
            else:
                ffn1()
        def ffn_part2():
            xout = resids[state["ri"]]; state["ri"] += 1
            residual_pass(state["xin"], xout, do_next_stats=(layer == 0))
            if layer == 0:
                stats_finalize()
                ln_apply(xout)
            state["xin"] = xout
        return [attn_part1, attn_part2, attn_part3, attn_part4, ffn_part, ffn_part2]

    steps += mk_layer(0)
    steps += mk_layer(1)
    for i, st_fn in enumerate(steps):
        if i < DBG_N:
            st_fn()

    E.soft_barrier()
    o("G", lambda h: h.memset(af(EF, 1), 0.0), reads=(), writes=[("fin",)])

    blk = ctx.enter_context(nc.Block())

    @blk.tensor
    def _(e):
        E.replay_engine("T", e, sems)

    @blk.vector
    def _(e):
        E.replay_engine("V", e, sems)

    @blk.scalar
    def _(e):
        E.replay_engine("S", e, sems)

    @blk.gpsimd
    def _(e):
        E.replay_engine("G", e, sems)

    @blk.sync
    def _(e):
        E.replay_engine("Y", e, sems)

    ctx.close()
    return nc


_CACHE = {}


def kernel(**ins):
    import ml_dtypes
    g = stage_inputs(ins)
    if "nc" not in _CACHE:
        _CACHE["nc"] = build_program()
    nc = _CACHE["nc"]

    bf16 = np.dtype(ml_dtypes.bfloat16)
    shared = dict(g)
    shared["identb"] = np.eye(128, dtype=np.float32).astype(bf16)
    cos, sin = _rope_tables()
    x = ins["L0_attn_hidden_states"]
    da0, dm0 = make_dmat(0)
    da1, dm1 = make_dmat(T)

    in_maps = []
    for c in range(8):
        b, half = c // 2, c % 2
        t0 = half * T
        m = dict(shared)
        m["x0"] = np.ascontiguousarray(x[b, t0:t0 + T, :].T)
        ct_ = cos[t0:t0 + T, :].T
        st_ = sin[t0:t0 + T, :].T
        m["cosmy"] = np.ascontiguousarray(np.concatenate([ct_, ct_], axis=0))
        m["sinmy"] = np.ascontiguousarray(np.concatenate([st_, st_], axis=0))
        m["dmat_a"] = da0 if half == 0 else da1
        m["dmat_m"] = dm0 if half == 0 else dm1
        in_maps.append(m)

    res = run_bass_kernel_spmd(nc, in_maps, core_ids=list(range(8)))
    out = np.empty((B, S, D), dtype=np.float32)
    for c in range(8):
        b, half = c // 2, c % 2
        out[b, half * T:(half + 1) * T, :] = res.results[c]["out"].T
    return out
